# version 08
# baseline (speedup 1.0000x reference)
import numpy as np
import ml_dtypes
import concourse.bass as bass
import concourse.mybir as mybir
from concourse.bass_utils import run_bass_kernel_spmd

F32 = mybir.dt.float32
BF16 = mybir.dt.bfloat16
AF = mybir.ActivationFunctionType
ALU = mybir.AluOpType
AX = mybir.AxisListType

NCORES = 8
NTOK = 2048
NTT = 16
NR = 1024
NRT = 8
D = 2048
HQ = 8
HKV = 8
HD = 128
F_EXP = 1024
E = 16
MAGIC = float(3 * 2.0 ** 22)
TWO_PI = float(2 * np.pi)
INV_2PI = float(1.0 / (2 * np.pi))
CHAINS = [["S"], [0, 1, 2], [3, 4, 5], [6, 7, 8], [9, 10, 11], [12, 13, 14], [15]]


class Ctr:
    def __init__(self, nc, es, name):
        self.h = es.enter_context(nc.semaphore(name))
        self.n = 0

    def inc(self, instr, k=1):
        instr.then_inc(self.h, k)
        self.n += k
        return self.n


def build_program(debug_stop=None):
    from contextlib import ExitStack
    nc = bass.Bass("TRN2", target_bir_lowering=False)
    es = ExitStack()

    def din(name, shape, dtype=F32):
        return nc.dram_tensor(name, list(shape), dtype, kind="ExternalInput").ap()

    xT_d = din("xT", (D, NTOK))
    xr_d = din("xr", (NR, D))
    wqT_d = din("wqT", (D, HQ * HD))
    wkT_d = din("wkT", (D, HKV * HD))
    wvT_d = din("wvT", (D, HKV * HD))
    woT_d = din("woT", (D, D))
    bq_d = din("bq", (1, HQ * HD))
    bk_d = din("bk", (1, HKV * HD))
    bv_d = din("bv", (1, HKV * HD))
    bo_d = din("bo", (1, D))
    rT2_d = din("rT2", (128, 128))
    rwT_d = din("rwT", (D, E))
    gw_d = din("gw", (E, D, F_EXP))
    uw_d = din("uw", (E, D, F_EXP))
    dw_d = din("dw", (E, F_EXP, D))
    sgw_d = din("sgw", (D, D))
    suw_d = din("suw", (D, D))
    sdw_d = din("sdw", (D, D))
    cos_d = din("cos_t", (NTOK, HD))
    srot_d = din("srot_t", (NTOK, HD))
    ident_d = din("ident", (128, 128), BF16)
    ones_d = din("ones_r", (1, 128), BF16)
    eind_d = din("e_ind", (E, E * 128), BF16)
    cc_d = din("cconst", (128, 8))
    out_d = nc.dram_tensor("out", [NR, D], F32, kind="ExternalOutput").ap()

    sbt = nc.sbuf_tensor
    ident = es.enter_context(sbt("ident_s", [128, 128], BF16))
    ones_r = es.enter_context(sbt("ones_s", [1, 128], BF16))
    cc = es.enter_context(sbt("cc", [128, 8], F32))
    rT2_sb = es.enter_context(sbt("rT2_sb", [128, 128], BF16))
    rwT_sb = es.enter_context(sbt("rwT_sb", [128, 16 * E], BF16))
    A_T = es.enter_context(sbt("A_T", [128, 128], BF16))
    ZB = es.enter_context(sbt("ZB", [128, 256], F32))
    REG_G = es.enter_context(sbt("REG_G", [128, 4096], BF16))

    ACCR = es.enter_context(sbt("ACCR", [128, 16384], BF16))
    R1 = es.enter_context(sbt("R1", [128, 16384], BF16))
    R2 = es.enter_context(sbt("R2", [128, 16384], BF16))
    R3 = es.enter_context(sbt("R3", [128, 16384], BF16))
    R4 = es.enter_context(sbt("R4", [128, 16384], BF16))
    R5 = es.enter_context(sbt("R5", [128, 16384], BF16))

    kp = ACCR
    ACC = ACCR
    xTh = R1
    qp = R1
    sqb1 = R1[:, 0:2048]
    outf = R1[:, :].bitcast(F32)
    q_tok = R2
    R2F = R2[:, :].bitcast(F32)
    rrPa = R2F[:, 0:1024]
    rrAa = R2F[:, 1024:2048]
    rrBa = R2F[:, 2048:3072]
    rrPb = R2F[:, 3072:4096]
    rrAb = R2F[:, 4096:5120]
    rrBb = R2F[:, 5120:6144]
    OUTt = R2F[:, 2048:3072]
    pscr = [R2[:, 6144 + i * 1024: 6144 + (i + 1) * 1024] for i in range(2)]
    x1T = R2
    k_tok = R3
    wo_ring = [R3[:, i * 512:(i + 1) * 512] for i in range(8)]
    x_ring = [R3[:, 4096 + i * 2048: 4096 + (i + 1) * 2048] for i in range(2)]
    bo_sb = R3[0:1, 8192:10240]
    sqb2 = R3[:, 0:2048]
    v_tok = R4
    gu_slot = [R4[:, 8192 + i * 2048: 8192 + (i + 1) * 2048] for i in range(4)]
    wstage = R5
    cos_sb = R5[:, 0:2048]
    srot_sb = R5[:, 4096:6144]
    outT = R5
    outTv = R5[:, :].rearrange("p (ti r) -> p ti r", ti=16)
    dw_slot = [R5[:, i * 4096:(i + 1) * 4096] for i in range(2)]
    wbc_sb = R5[:, 8192:9216]
    wrt = R5[0:16, 9216:10240]
    eind = R5[0:16, 10240:12288]
    mrot2 = [REG_G[:, 0:1024], REG_G[:, 1024:2048]]
    sqn2 = [REG_G[:, 2048:3072], REG_G[:, 3072:4096]]
    ocast = REG_G[:, 0:1024]
    sqb1g = REG_G[:, 1024:3072]
    A_sb = REG_G[0:8, 2048:3072]
    wcast = REG_G[:, 3072:3088]

    POOL = es.enter_context(nc.psum_tensor("POOL", [128, 4096], F32))

    def bank(i):
        return POOL[:, i * 512:(i + 1) * 512]

    TRB = [bank(7).bitcast(BF16)[:, 0:128], bank(2).bitcast(BF16)[:, 0:128]]
    psATs = [bank(4).bitcast(BF16)[:, 0:64], bank(3).bitcast(BF16)[:, 0:64]]
    psA0 = POOL[0:8, 3072:3584]
    psA1 = POOL[0:8, 2560:3072]

    dW = Ctr(nc, es, "dW")
    dS = Ctr(nc, es, "dS")
    cT = Ctr(nc, es, "cT")
    cV = Ctr(nc, es, "cV")
    cS = Ctr(nc, es, "cS")
    cP = Ctr(nc, es, "cP")

    t_, v_, s_, g_, y_ = nc.tensor, nc.vector, nc.scalar, nc.gpsimd, nc.sync

    def gdma(dst, src, wait=None):
        if wait is not None:
            g_.wait_ge(wait[0].h, wait[1])
        return dW.inc(g_.dma_start(dst, src), 16)

    def r3ap(dram_ap, p=128):
        return dram_ap.rearrange("(c p) n -> p c n", p=p)

    def c3(sbuf_ap, c):
        return sbuf_ap.rearrange("p (c n) -> p c n", c=c)

    def mmul(out, lhsT, rhs, start, stop):
        return t_.matmul(out, lhsT, rhs, start=start, stop=stop)

    def dump_and_finish(src_fn, gate_ctr, gate_val):
        g_.wait_ge(gate_ctr.h, gate_val)
        fin = 0
        for rt in range(NRT):
            fin = dW.inc(g_.dma_start(out_d[rt * 128:(rt + 1) * 128, :],
                                      src_fn(rt)), 16)
        g_.wait_ge(dW.h, fin)
        return nc, es

    gdma(ident[:, :], ident_d)
    gdma(ones_r[:, :], ones_d)
    gdma(cc[:, :], cc_d)
    gdma(rT2_sb[:, :], rT2_d)
    gdma(c3(rwT_sb[:, :], 16), r3ap(rwT_d))

    PIECES = ([(wqT_d[:, og * 512:(og + 1) * 512], "q", og) for og in range(2)]
              + [(wkT_d[:, og * 512:(og + 1) * 512], "k", og) for og in range(2)]
              + [(wvT_d[:, og * 512:(og + 1) * 512], "v", og) for og in range(2)])
    bq_sb = REG_G[0:1, 0:1024]
    bk_sb = REG_G[0:1, 1024:2048]
    bv_sb = REG_G[0:1, 2048:3072]
    gdma(bq_sb, bq_d)
    gdma(bk_sb, bk_d)
    gdma(bv_sb, bv_d)

    xT_wait = [None, None]
    xT_wait[0] = gdma(c3(xTh[:, :], 16), r3ap(xT_d[:, 0:1024]))
    qkv_wait = {}

    def load_piece(sh, i, credit):
        half = wstage[:, (i % 2) * 8192:(i % 2) * 8192 + 8192]
        qkv_wait[(sh, i)] = gdma(c3(half, 16), r3ap(PIECES[i][0]), wait=credit)

    load_piece(0, 0, None)
    load_piece(0, 1, None)

    psum_credit = {}
    for sh in range(2):
        t_.wait_ge(dW.h, xT_wait[sh])
        for i, (dsrc, kind, og) in enumerate(PIECES):
            half = wstage[:, (i % 2) * 8192:(i % 2) * 8192 + 8192]
            t_.wait_ge(dW.h, qkv_wait[(sh, i)])
            brow = (bq_sb if kind == "q" else bk_sb if kind == "k" else bv_sb)[
                0:1, og * 512:(og + 1) * 512]
            for tile in range(8):
                gt = sh * 8 + tile
                slot = tile % 2
                pq = bank(slot)
                need = psum_credit.get(("p01", slot))
                if need is not None:
                    t_.wait_ge(cS.h, need)
                for k in range(16):
                    mmul(pq, xTh[:, k * 1024 + tile * 128: k * 1024 + tile * 128 + 128],
                         half[:, k * 512:(k + 1) * 512], start=(k == 0), stop=False)
                te_n = cT.inc(mmul(pq, ones_r[0:1, :], brow, start=False, stop=True))
                s_.wait_ge(cT.h, te_n)
                dst = (q_tok if kind == "q" else k_tok if kind == "k" else v_tok)[
                    :, gt * 1024 + og * 512: gt * 1024 + og * 512 + 512]
                psum_credit[("p01", slot)] = cS.inc(s_.copy(dst, pq))
            if sh == 0 and i + 2 < 6:
                load_piece(0, i + 2, (cT, cT.n))
            elif sh == 0:
                load_piece(1, i + 2 - 6, (cT, cT.n))
            elif i + 2 < 6:
                load_piece(1, i + 2, (cT, cT.n))
        if sh == 0:
            xT_wait[1] = gdma(c3(xTh[:, :], 16), r3ap(xT_d[:, 1024:2048]),
                              wait=(cT, cT.n))
    qkv_done_t = cT.n
    qkv_done_sc = cS.n
    gdma(c3(cos_sb, 16), r3ap(cos_d), wait=(cT, qkv_done_t))
    tbl_wait = gdma(c3(srot_sb, 16), r3ap(srot_d))

    if debug_stop == 1:
        return dump_and_finish(lambda rt: q_tok[:, rt * 2048: rt * 2048 + 2048],
                               cS, qkv_done_sc)

    tr_credit = [None, None]
    rope_prev = [0, 0]
    v_.wait_ge(cS.h, qkv_done_sc)
    v_.wait_ge(dW.h, tbl_wait)
    g_.wait_ge(cS.h, qkv_done_sc)
    g_.wait_ge(dW.h, tbl_wait)

    def rope_stage_a(tile, kind):
        nh = 8
        w = 1024
        src = (q_tok if kind == "q" else k_tok)[:, tile * 1024:(tile + 1) * 1024]
        src3 = src.rearrange("p (h d) -> p h d", h=nh)
        cosb = cos_sb[:, tile * HD:(tile + 1) * HD]
        srotb = srot_sb[:, tile * HD:(tile + 1) * HD]
        half = 0 if kind == "q" else 1
        m3 = mrot2[half].rearrange("p (h d) -> p h d", h=nh)
        qcos = sqn2[half].rearrange("p (h d) -> p h d", h=nh)
        g_.wait_ge(cV.h, rope_prev[half])
        g_.tensor_tensor(m3[:, :, 0:64], src3[:, :, 64:128],
                         srotb[:, 0:64].unsqueeze(1).broadcast_to((128, nh, 64)), ALU.mult)
        np2 = cP.inc(g_.tensor_tensor(m3[:, :, 64:128], src3[:, :, 0:64],
                     srotb[:, 64:128].unsqueeze(1).broadcast_to((128, nh, 64)), ALU.mult))
        n3 = cV.inc(v_.tensor_tensor(qcos, src3,
                                     cosb.unsqueeze(1).broadcast_to((128, nh, HD)), ALU.mult))
        return (src, src3, half, np2, n3)

    def rope_stage_b(st):
        src, src3, half, np2, n3 = st
        v_.wait_ge(cV.h, n3)
        v_.wait_ge(cP.h, np2)
        n4 = cV.inc(v_.tensor_tensor(src, sqn2[half], mrot2[half], ALU.add))
        rope_prev[half] = n4
        return (src, src3, half, n4)

    def norm_stage_a(st):
        src, src3, half, n4 = st
        zb0 = 0 if half == 0 else 8
        n_sq = None
        for h in range(8):
            n_sq = v_.scalar_tensor_tensor(
                sqn2[half][:, h * 128:(h + 1) * 128], src3[:, h, :], 1.0,
                src3[:, h, :], ALU.mult, ALU.mult,
                accum_out=ZB[:, zb0 + h: zb0 + h + 1])
        n_sq = cV.inc(n_sq)
        return (src, src3, half, n_sq)

    def norm_stage_b(st):
        src, src3, half, n_sq = st
        zb0 = 0 if half == 0 else 8
        v_.wait_ge(cV.h, n_sq)
        n_cl = cV.inc(v_.tensor_scalar(ZB[:, 32 + zb0:40 + zb0], ZB[:, zb0:zb0 + 8],
                                       1e-12, None, ALU.max))
        s_.wait_ge(cV.h, n_cl)
        s_.activation(ZB[:, 64 + zb0:72 + zb0], ZB[:, 32 + zb0:40 + zb0], AF.Ln)
        n_r = cS.inc(s_.activation(ZB[:, 32 + zb0:40 + zb0], ZB[:, 64 + zb0:72 + zb0],
                                   AF.Exp, scale=cc[:, 0:1]))
        v_.wait_ge(cS.h, n_r)
        n_hat = cV.inc(v_.tensor_tensor(
            src3, src3,
            ZB[:, 32 + zb0:40 + zb0].unsqueeze(2).broadcast_to((128, 8, 128)),
            ALU.mult))
        return (src, src3, half, n_hat)

    def trans_stage(st, tile):
        src, src3, half, n_hat = st
        t_.wait_ge(cV.h, n_hat)
        dstb = qp if half == 0 else kp
        for h in range(8):
            sl = h % 2
            if tr_credit[sl] is not None:
                t_.wait_ge(cS.h, tr_credit[sl])
            te1 = cT.inc(t_.transpose(TRB[sl], src3[:, h, :], ident[:, :]))
            s_.wait_ge(cT.h, te1)
            tr_credit[sl] = cS.inc(
                s_.copy(dstb[:, h * 2048 + tile * 128: h * 2048 + tile * 128 + 128],
                        TRB[sl]))

    for tile in range(NTT):
        sq = rope_stage_a(tile, "q")
        sk = rope_stage_a(tile, "k")
        sq = rope_stage_b(sq)
        sk = rope_stage_b(sk)
        sq = norm_stage_a(sq)
        sk = norm_stage_a(sk)
        sq = norm_stage_b(sq)
        sk = norm_stage_b(sk)
        trans_stage(sq, tile)
        trans_stage(sk, tile)
    qhat_done_sc = cS.n

    if debug_stop == 21:
        return dump_and_finish(lambda rt: qp[:, rt * 2048: rt * 2048 + 2048],
                               cS, qhat_done_sc)

    t_.wait_ge(cS.h, qhat_done_sc)
    fm_banks = [(6, 5), (4, 3)]
    fm_sets = [(rrPa, rrAa, rrBa), (rrPb, rrAb, rrBb)]
    fm_vec_done = [None, None]
    fm_sc_done = [None, None]
    gi = 0
    for base in (qp, kp):
        for h in range(8):
            for gp2 in range(2):
                col = h * 2048 + gp2 * 1024
                sel = gi % 2
                bA, bB = fm_banks[sel]
                rp, ra, rb = fm_sets[sel]
                if fm_sc_done[sel] is not None:
                    t_.wait_ge(cS.h, fm_sc_done[sel])
                mmul(bank(bA), rT2_sb[:, :], base[:, col:col + 512],
                     start=True, stop=True)
                te = cT.inc(mmul(bank(bB), rT2_sb[:, :], base[:, col + 512:col + 1024],
                                 start=True, stop=True))
                s_.wait_ge(cT.h, te)
                if fm_vec_done[sel] is not None:
                    s_.wait_ge(cV.h, fm_vec_done[sel])
                s_.activation(rp[:, 0:512], bank(bA), AF.Identity, bias=cc[:, 2:3])
                nev = cS.inc(s_.activation(rp[:, 512:1024], bank(bB), AF.Identity,
                                           bias=cc[:, 2:3]))
                fm_sc_done[sel] = nev
                v_.wait_ge(cS.h, nev)
                nv0 = cV.inc(v_.tensor_scalar(ra, rp, INV_2PI, MAGIC,
                                              ALU.mult, ALU.add))
                v_.wait_ge(cV.h, nv0)
                nv1 = cV.inc(v_.tensor_scalar(rb, ra, MAGIC, TWO_PI,
                                              ALU.subtract, ALU.mult))
                v_.wait_ge(cV.h, nv1)
                nv2 = cV.inc(v_.tensor_tensor(ra, rp, rb, ALU.subtract))
                fm_vec_done[sel] = nv2
                s_.wait_ge(cV.h, nv2)
                cS.inc(s_.activation(base[:, col:col + 1024], ra, AF.Sin))
                gi += 1
    featmap_sc = cS.n

    if debug_stop == 2:
        return dump_and_finish(lambda rt: qp[:, rt * 2048: rt * 2048 + 2048],
                               cS, featmap_sc)

    s_.wait_ge(cS.h, featmap_sc)
    t_.wait_ge(cS.h, featmap_sc)
    v_.wait_ge(cS.h, featmap_sc)
    prev_scA = None
    prev_mm2 = {0: None, 1: None}
    prev_gp2 = {0: None, 1: None}
    for tile in range(NTT):
        ats = tile % 2
        last = None
        for half in range(2):
            pA = psA0 if half == 0 else psA1
            for tl in range(64):
                tglob = tile * 128 + half * 64 + tl
                if half == 0 and tl == 0 and prev_scA is not None:
                    t_.wait_ge(cS.h, prev_scA)
                qsl = qp[:, tglob: tglob + 7 * 2048 + 1: 2048]
                ksl = kp[:, tglob: tglob + 7 * 2048 + 1: 2048]
                outap = pA[:, tl: tl + 7 * 64 + 1: 64]
                last = mmul(outap, qsl, ksl, start=True, stop=True)
        teA = cT.inc(last)
        s_.wait_ge(cT.h, teA)
        s_.activation(A_sb[:, 0:512], psA0, AF.Copy, scale=cc[0:8, 3:4])
        prev_scA = cS.inc(s_.activation(A_sb[:, 512:1024], psA1, AF.Copy,
                                        scale=cc[0:8, 3:4]))
        t_.wait_ge(cS.h, prev_scA)
        if prev_mm2[ats] is not None:
            t_.wait_ge(cV.h, prev_mm2[ats])
            t_.wait_ge(cP.h, prev_gp2[ats])
        lastT = None
        for half in range(2):
            for hp in range(HKV):
                lastT = t_.transpose(
                    psATs[ats][half * 64: half * 64 + 64, hp * 8:(hp + 1) * 8],
                    A_sb[:, half * 512 + hp * 64: half * 512 + (hp + 1) * 64],
                    ident[0:8, 0:8])
        teT = cT.inc(lastT)
        v_.wait_ge(cT.h, teT)
        atv = A_T[:, ats * 64:(ats + 1) * 64]
        nvat = cV.inc(v_.tensor_copy(atv, psATs[ats]))
        v_.wait_ge(cV.h, nvat)
        at3 = atv.rearrange("p (hp h) -> p hp h", hp=HKV)
        prodn = [None, None]
        addn = {}
        for hp in range(HKV):
            sl2 = hp % 2
            if hp >= 2:
                v_.wait_ge(cV.h, addn[hp - 2])
            vb = v_tok[:, tile * 1024 + hp * 128: tile * 1024 + hp * 128 + 128]
            prodn[sl2] = cV.inc(v_.tensor_tensor(
                pscr[sl2].rearrange("p (h d) -> p h d", h=HQ),
                vb.unsqueeze(1).broadcast_to((128, HQ, 128)),
                at3[:, hp, :].unsqueeze(2).broadcast_to((128, HQ, 128)),
                ALU.mult))
            v_.wait_ge(cV.h, prodn[sl2])
            if hp == 0:
                addn[hp] = cV.inc(v_.tensor_copy(OUTt, pscr[sl2]))
            else:
                v_.wait_ge(cV.h, addn[hp - 1])
                addn[hp] = cV.inc(v_.tensor_tensor(OUTt, OUTt, pscr[sl2], ALU.add))
        edge = addn[HKV - 1]
        prev_mm2[ats] = edge
        prev_gp2[ats] = cP.n
        v_.wait_ge(cV.h, edge)
        ncast = cV.inc(v_.tensor_copy(ocast, OUTt))
        t_.wait_ge(cV.h, ncast)
        for h in range(HQ):
            sl = h % 2
            if tr_credit[sl] is not None:
                t_.wait_ge(cS.h, tr_credit[sl])
            te2 = cT.inc(t_.transpose(TRB[sl], ocast[:, h * 128:(h + 1) * 128],
                                      ident[:, :]))
            s_.wait_ge(cT.h, te2)
            tr_credit[sl] = cS.inc(s_.copy(
                outTv[:, :, h * 128 + tile * 8: h * 128 + tile * 8 + 8],
                TRB[sl].rearrange("p (tb ti) -> p ti tb", tb=8)))
    outT_sc = cS.n
    phase3_v = cV.n
    phase3_t = cT.n

    if debug_stop == 3:
        return dump_and_finish(lambda rt: outT[:, rt * 2048: rt * 2048 + 2048],
                               cS, outT_sc)

    wo_wait = {}
    wo_slot_credit = {}
    pieces4 = [(oq, k) for oq in range(4) for k in range(16)]
    np_issued = 0

    def issue_wo(n):
        nonlocal np_issued
        while np_issued < min(n, 64):
            oq, k = pieces4[np_issued]
            slot = np_issued % 8
            credit = wo_slot_credit.get(slot)
            if np_issued == 0:
                credit = (cS, qhat_done_sc)
            wo_wait[(oq, k)] = gdma(
                wo_ring[slot][:, :],
                woT_d[k * 128:(k + 1) * 128, oq * 512:(oq + 1) * 512], wait=credit)
            np_issued += 1

    issue_wo(8)
    bo_wait = gdma(bo_sb, bo_d)
    wo_bank_credit = [
        (cS, psum_credit.get(("p01", 0), 0)), (cS, psum_credit.get(("p01", 1), 0)),
        (cS, outT_sc), (cS, 0), (cV, phase3_v), (cS, prev_scA or 0),
        (cS, prev_scA or 0), (cS, outT_sc)]
    for oq in range(4):
        tile_done = [None] * NRT
        for k in range(16):
            t_.wait_ge(dW.h, wo_wait[(oq, k)])
            if k == 0:
                for b8 in range(8):
                    need = wo_bank_credit[b8]
                    if need is not None:
                        t_.wait_ge(need[0].h, need[1])
                        wo_bank_credit[b8] = None
            lastk = None
            for rt in range(NRT):
                lastk = mmul(bank(rt),
                             outT[:, k * 1024 + rt * 128: k * 1024 + rt * 128 + 128],
                             wo_ring[(oq * 16 + k) % 8][:, :],
                             start=(k == 0), stop=False)
            wo_slot_credit[(oq * 16 + k) % 8] = (cT, cT.inc(lastk))
            issue_wo(oq * 16 + k + 9)
        t_.wait_ge(dW.h, bo_wait)
        for rt in range(NRT):
            tile_done[rt] = cT.inc(
                mmul(bank(rt), ones_r[0:1, :], bo_sb[0:1, oq * 512:(oq + 1) * 512],
                     start=False, stop=True))
        for rt in range(NRT):
            s_.wait_ge(cT.h, tile_done[rt])
            nsc = cS.inc(
                s_.copy(ACC[:, rt * 2048 + oq * 512: rt * 2048 + oq * 512 + 512],
                        bank(rt)))
            wo_bank_credit[rt] = (cS, nsc)
    acc_attn_sc = cS.n
    wo_done_t = cT.n

    x_wait = [None] * NRT
    x_slot_credit = [(cV, phase3_v), (cV, phase3_v)]

    def issue_x(rt):
        x_wait[rt] = gdma(x_ring[rt % 2][:, :],
                          xr_d[rt * 128:(rt + 1) * 128, :],
                          wait=x_slot_credit[rt % 2])

    issue_x(0)
    issue_x(1)
    v_.wait_ge(cS.h, acc_attn_sc)
    pl = POOL[:, 2048:2064]
    pl_credit = None
    router_done_sc = None
    for rt in range(NRT):
        at = ACC[:, rt * 2048:(rt + 1) * 2048]
        v_.wait_ge(dW.h, x_wait[rt])
        nres = cV.inc(v_.tensor_tensor(at, at, x_ring[rt % 2][:, :], ALU.add))
        x_slot_credit[rt % 2] = (cV, nres)
        if rt + 2 < NRT:
            issue_x(rt + 2)
        s_.wait_ge(cV.h, nres)
        nsq = cS.inc(s_.activation(sqb1, at, AF.Square))
        v_.wait_ge(cS.h, nsq)
        v_.tensor_reduce(ZB[:, 0:1], at, AX.X, ALU.add)
        nrd = cV.inc(v_.tensor_reduce(ZB[:, 1:2], sqb1, AX.X, ALU.add))
        v_.wait_ge(cV.h, nrd)
        v_.tensor_scalar(ZB[:, 2:3], ZB[:, 0:1], 1.0 / D, None, ALU.mult)
        nv1 = cV.inc(v_.tensor_scalar(ZB[:, 3:4], ZB[:, 1:2], 1.0 / D, None, ALU.mult))
        v_.wait_ge(cV.h, nv1)
        nv2 = cV.inc(v_.tensor_tensor(ZB[:, 4:5], ZB[:, 2:3], ZB[:, 2:3], ALU.mult))
        v_.wait_ge(cV.h, nv2)
        nvar = cV.inc(v_.tensor_tensor(ZB[:, 5:6], ZB[:, 3:4], ZB[:, 4:5], ALU.subtract))
        s_.wait_ge(cV.h, nvar)
        s_.activation(ZB[:, 6:7], ZB[:, 5:6], AF.Ln, bias=cc[:, 4:5])
        nrstd = cS.inc(s_.activation(ZB[:, 7:8], ZB[:, 6:7], AF.Exp, scale=cc[:, 0:1]))
        v_.wait_ge(cS.h, nrstd)
        nx1 = cV.inc(v_.tensor_scalar(at, at, ZB[:, 2:3], ZB[:, 7:8],
                                      ALU.subtract, ALU.mult))
        t_.wait_ge(cV.h, nx1)
        for c in range(16):
            sl = c % 2
            if tr_credit[sl] is not None:
                t_.wait_ge(cS.h, tr_credit[sl])
            te2 = cT.inc(t_.transpose(TRB[sl], at[:, c * 128:(c + 1) * 128],
                                      ident[:, :]))
            s_.wait_ge(cT.h, te2)
            tr_credit[sl] = cS.inc(
                s_.copy(x1T[:, c * 1024 + rt * 128: c * 1024 + rt * 128 + 128],
                        TRB[sl]))
        t_.wait_ge(cS.h, tr_credit[1])
        if pl_credit is not None:
            t_.wait_ge(cV.h, pl_credit)
        tel = None
        for c in range(16):
            tel = mmul(pl, x1T[:, c * 1024 + rt * 128: c * 1024 + rt * 128 + 128],
                       rwT_sb[:, c * 16:(c + 1) * 16], start=(c == 0), stop=(c == 15))
        tel = cT.inc(tel)
        v_.wait_ge(cT.h, tel)
        lg = ZB[:, 16:32]
        nlg = cV.inc(v_.tensor_copy(lg, pl))
        pl_credit = nlg
        v_.wait_ge(cV.h, nlg)
        nl1 = cV.inc(v_.reduce_max(ZB[:, 32:33], lg, axis=AX.X))
        v_.wait_ge(cV.h, nl1)
        nm1 = cV.inc(v_.tensor_scalar(ZB[:, 48:64], lg, ZB[:, 32:33], None, ALU.is_ge))
        v_.wait_ge(cV.h, nm1)
        nt1 = cV.inc(v_.scalar_tensor_tensor(ZB[:, 64:80], ZB[:, 48:64], -1e30, lg,
                                             ALU.mult, ALU.add))
        v_.wait_ge(cV.h, nt1)
        nl2 = cV.inc(v_.reduce_max(ZB[:, 33:34], ZB[:, 64:80], axis=AX.X))
        v_.wait_ge(cV.h, nl2)
        v_.tensor_scalar(ZB[:, 80:96], lg, ZB[:, 33:34], None, ALU.is_ge)
        ndl = cV.inc(v_.tensor_tensor(ZB[:, 34:35], ZB[:, 32:33], ZB[:, 33:34],
                                      ALU.subtract))
        v_.wait_ge(cV.h, ndl)
        nm2 = cV.inc(v_.tensor_tensor(ZB[:, 96:112], ZB[:, 80:96], ZB[:, 48:64],
                                      ALU.subtract))
        s_.wait_ge(cV.h, nm2)
        nw1 = cS.inc(s_.activation(ZB[:, 35:36], ZB[:, 34:35], AF.Sigmoid))
        v_.wait_ge(cS.h, nw1)
        nw2 = cV.inc(v_.tensor_scalar(ZB[:, 36:37], ZB[:, 35:36], -1.0, 1.0,
                                      ALU.mult, ALU.add))
        v_.wait_ge(cV.h, nw2)
        nwa = cV.inc(v_.tensor_scalar(ZB[:, 64:80], ZB[:, 48:64], ZB[:, 35:36],
                                      None, ALU.mult))
        v_.wait_ge(cV.h, nwa)
        nwr = cV.inc(v_.scalar_tensor_tensor(ZB[:, 80:96], ZB[:, 96:112], ZB[:, 36:37],
                                             ZB[:, 64:80], ALU.mult, ALU.add))
        v_.wait_ge(cV.h, nwr)
        nwb = cV.inc(v_.tensor_copy(wcast, ZB[:, 80:96]))
        t_.wait_ge(cV.h, nwb)
        if tr_credit[0] is not None:
            t_.wait_ge(cS.h, tr_credit[0])
        tew = cT.inc(t_.transpose(TRB[0][0:16, :], wcast, ident[:, :]))
        s_.wait_ge(cT.h, tew)
        tr_credit[0] = cS.inc(
            s_.copy(wrt[0:16, rt * 128:(rt + 1) * 128], TRB[0][0:16, 0:128]))
        router_done_sc = tr_credit[0]
    ln1_done_v = cV.n
    x1T_done_sc = cS.n
    eind_wait = gdma(eind[:, :], eind_d, wait=(cT, wo_done_t))

    if debug_stop == 4:
        return dump_and_finish(lambda rt: ACC[:, rt * 2048:(rt + 1) * 2048],
                               cS, x1T_done_sc)

    t_.wait_ge(cS.h, x1T_done_sc)
    t_.wait_ge(dW.h, eind_wait)
    sil = REG_G[:, 1024:1536]

    def hw_ap(pos, hh):
        if pos < 2:
            return R3[:, (pos * 8 + hh) * 1024:(pos * 8 + hh + 1) * 1024]
        return R4[:, hh * 1024:(hh + 1) * 1024]

    hw_first_guard = {"R3": (cT, wo_done_t), "R4": (cV, phase3_v)}

    GUSEQ = []
    for C in CHAINS:
        for e in C:
            for hh in range(16 if e == "S" else 8):
                GUSEQ.append((e, hh))
    gu_wait = {}
    gu_pair_te = {}

    def issue_gu_pair(i):
        if i >= len(GUSEQ):
            return
        e, hh = GUSEQ[i]
        basep = (2 * i) % 4
        credit = gu_pair_te.get(i - 2)
        if credit is None:
            credit = (cV, phase3_v)
        for j, wsel in enumerate(("g", "u")):
            if e == "S":
                srcw = (sgw_d if wsel == "g" else suw_d)[:, hh * 128:(hh + 1) * 128]
            else:
                srcw = (gw_d if wsel == "g" else uw_d)[e, :, hh * 128:(hh + 1) * 128]
            gu_wait[(e, hh, wsel)] = (basep + j, gdma(
                c3(gu_slot[basep + j], 16), r3ap(srcw),
                wait=(credit if j == 0 else None)))

    DWSEQ = []
    for C in CHAINS:
        for oq in range(4):
            for e in C:
                for hseg in range(2 if e == "S" else 1):
                    DWSEQ.append((e, oq, hseg))
    dw_wait = {}
    dw_seq_te = {}
    dw_idx_of = {key: i for i, key in enumerate(DWSEQ)}

    def issue_dw_piece(i):
        if i >= len(DWSEQ):
            return
        e, oq, hseg = DWSEQ[i]
        slot = i % 2
        credit = dw_seq_te.get(i - 2)
        if credit is None:
            credit = (cT, wo_done_t)
        if e == "S":
            srcw = sdw_d[hseg * 1024:(hseg + 1) * 1024, oq * 512:(oq + 1) * 512]
        else:
            srcw = dw_d[e, :, oq * 512:(oq + 1) * 512]
        dw_wait[(e, oq, hseg)] = (slot, gdma(c3(dw_slot[slot], 8), r3ap(srcw),
                                             wait=credit))

    issue_gu_pair(0)
    issue_gu_pair(1)
    issue_dw_piece(0)
    issue_dw_piece(1)

    ln2_gate = [None] * NRT
    down_bank_credit = [(cS, acc_attn_sc)] * 8
    down_bank_credit[4] = (cV, pl_credit or 0)
    down_bank_credit[2] = (cS, x1T_done_sc)
    down_bank_credit[7] = (cS, x1T_done_sc)
    gu_pair_idx = 0

    for ci, C in enumerate(CHAINS):
        for pos, e in enumerate(C):
            nhh = 16 if e == "S" else 8
            if e != "S":
                for gk in range(2):
                    t_.wait_ge(cS.h, router_done_sc)
                    need = down_bank_credit[4]
                    if need is not None:
                        t_.wait_ge(need[0].h, need[1])
                    tew = cT.inc(mmul(POOL[:, 2048:2560], eind[:, e * 128:(e + 1) * 128],
                                      wrt[0:16, gk * 512:(gk + 1) * 512],
                                      start=True, stop=True))
                    s_.wait_ge(cT.h, tew)
                    nwc = cS.inc(s_.copy(wbc_sb[:, gk * 512:(gk + 1) * 512],
                                         POOL[:, 2048:2560]))
                    down_bank_credit[4] = (cS, nwc)
            gkey = "R4" if (e != "S" and pos == 2) else "R3"
            guard = hw_first_guard.get(gkey)
            if guard is not None:
                v_.wait_ge(guard[0].h, guard[1])
                hw_first_guard[gkey] = None
            for hh in range(nhh):
                gslot, gw_w = gu_wait[(e, hh, "g")]
                uslot, uw_w = gu_wait[(e, hh, "u")]
                t_.wait_ge(dW.h, max(gw_w, uw_w))
                for gk in range(2):
                    pg = bank(gk)
                    pu = bank(2 + gk)
                    for bsel in (gk, 2 + gk):
                        need = down_bank_credit[bsel]
                        if need is not None:
                            t_.wait_ge(need[0].h, need[1])
                            down_bank_credit[bsel] = None
                    teg = None
                    for k in range(16):
                        mmul(pg, gu_slot[gslot][:, k * 128:(k + 1) * 128],
                             x1T[:, k * 1024 + gk * 512: k * 1024 + gk * 512 + 512],
                             start=(k == 0), stop=(k == 15))
                    for k in range(16):
                        teg = mmul(pu, gu_slot[uslot][:, k * 128:(k + 1) * 128],
                                   x1T[:, k * 1024 + gk * 512: k * 1024 + gk * 512 + 512],
                                   start=(k == 0), stop=(k == 15))
                    teg = cT.inc(teg)
                    s_.wait_ge(cT.h, teg)
                    nsil = cS.inc(s_.activation(sil, pg, AF.Silu))
                    down_bank_credit[gk] = (cS, nsil)
                    v_.wait_ge(cS.h, nsil)
                    hcols = hw_ap(pos if e != "S" else hh // 8,
                                  hh if e != "S" else hh % 8)[:, gk * 512:(gk + 1) * 512]
                    nh1 = cV.inc(v_.tensor_tensor(hcols, sil, pu, ALU.mult))
                    down_bank_credit[2 + gk] = (cV, nh1)
                    if e != "S":
                        v_.wait_ge(cV.h, nh1)
                        cV.inc(v_.tensor_tensor(hcols, hcols,
                                                wbc_sb[:, gk * 512:(gk + 1) * 512],
                                                ALU.mult))
                gu_pair_te[gu_pair_idx] = (cT, cT.n)
                issue_gu_pair(gu_pair_idx + 2)
                gu_pair_idx += 1
        nhw_done = cV.n
        for oq in range(4):
            tile_mm_done = [None] * 8
            for pi, e in enumerate(C):
                nhh = 16 if e == "S" else 8
                first_member = (pi == 0)
                last_member = (pi == len(C) - 1)
                seg_last = {}
                for hh in range(nhh):
                    hseg = hh // 8 if e == "S" else 0
                    slot, wv = dw_wait[(e, oq, hseg)]
                    t_.wait_ge(dW.h, wv)
                    if first_member and hh == 0:
                        t_.wait_ge(cV.h, nhw_done)
                        for b8 in range(8):
                            need = down_bank_credit[b8]
                            if need is not None:
                                t_.wait_ge(need[0].h, need[1])
                                down_bank_credit[b8] = None
                    rhs = dw_slot[slot][:, (hh % 8) * 512:((hh % 8) + 1) * 512]
                    lhs_pos = (hh // 8) if e == "S" else pi
                    lhs_hh = (hh % 8) if e == "S" else hh
                    for rt in range(NRT):
                        mmn = mmul(bank(rt),
                                   hw_ap(lhs_pos, lhs_hh)[:, rt * 128:(rt + 1) * 128],
                                   rhs,
                                   start=(first_member and hh == 0),
                                   stop=(last_member and hh == nhh - 1))
                        if last_member and hh == nhh - 1:
                            tile_mm_done[rt] = cT.inc(mmn)
                        else:
                            seg_last[hseg] = mmn
                    if (hh == nhh - 1) or (e == "S" and hh == 7):
                        idx = dw_idx_of[(e, oq, hseg)]
                        if last_member and hh == nhh - 1:
                            mdone = cT.n
                        else:
                            mdone = cT.inc(seg_last[hseg])
                        dw_seq_te[idx] = (cT, mdone)
                        issue_dw_piece(idx + 2)
            for rt in range(NRT):
                v_.wait_ge(cT.h, tile_mm_done[rt])
                nev = cV.inc(v_.tensor_tensor(
                    ACC[:, rt * 2048 + oq * 512: rt * 2048 + oq * 512 + 512],
                    ACC[:, rt * 2048 + oq * 512: rt * 2048 + oq * 512 + 512],
                    bank(rt), ALU.add))
                down_bank_credit[rt] = (cV, nev)
                if ci == len(CHAINS) - 1 and oq == 3:
                    ln2_gate[rt] = nev
        if debug_stop == 51 + ci:
            return dump_and_finish(lambda rt: ACC[:, rt * 2048:(rt + 1) * 2048],
                                   cV, cV.n)
    ffn_done_v = cV.n

    if debug_stop == 5:
        return dump_and_finish(lambda rt: ACC[:, rt * 2048:(rt + 1) * 2048],
                               cV, ffn_done_v)

    out_dma = [None] * NRT
    for rt in range(NRT):
        at = ACC[:, rt * 2048:(rt + 1) * 2048]
        s_.wait_ge(cV.h, ln2_gate[rt])
        nsq = cS.inc(s_.activation(sqb2, at, AF.Square))
        v_.wait_ge(cS.h, nsq)
        v_.tensor_reduce(ZB[:, 0:1], at, AX.X, ALU.add)
        nrd = cV.inc(v_.tensor_reduce(ZB[:, 1:2], sqb2, AX.X, ALU.add))
        v_.wait_ge(cV.h, nrd)
        v_.tensor_scalar(ZB[:, 2:3], ZB[:, 0:1], 1.0 / D, None, ALU.mult)
        nv1 = cV.inc(v_.tensor_scalar(ZB[:, 3:4], ZB[:, 1:2], 1.0 / D, None, ALU.mult))
        v_.wait_ge(cV.h, nv1)
        nv2 = cV.inc(v_.tensor_tensor(ZB[:, 4:5], ZB[:, 2:3], ZB[:, 2:3], ALU.mult))
        v_.wait_ge(cV.h, nv2)
        nvar = cV.inc(v_.tensor_tensor(ZB[:, 5:6], ZB[:, 3:4], ZB[:, 4:5], ALU.subtract))
        s_.wait_ge(cV.h, nvar)
        s_.activation(ZB[:, 6:7], ZB[:, 5:6], AF.Ln, bias=cc[:, 4:5])
        nrstd = cS.inc(s_.activation(ZB[:, 7:8], ZB[:, 6:7], AF.Exp, scale=cc[:, 0:1]))
        v_.wait_ge(cS.h, nrstd)
        slot = rt % 4
        if rt >= 4:
            v_.wait_ge(dS.h, out_dma[rt - 4])
        nout = cV.inc(v_.tensor_scalar(outf[:, slot * 2048:(slot + 1) * 2048], at,
                                       ZB[:, 2:3], ZB[:, 7:8], ALU.subtract, ALU.mult))
        y_.wait_ge(cV.h, nout)
        out_dma[rt] = dS.inc(y_.dma_start(out_d[rt * 128:(rt + 1) * 128, :],
                                          outf[:, slot * 2048:(slot + 1) * 2048]), 16)
    y_.wait_ge(dS.h, out_dma[NRT - 1])
    return nc, es


_BUILT = {}


def _get_program(debug_stop=None):
    if debug_stop not in _BUILT:
        _BUILT[debug_stop] = build_program(debug_stop)
    return _BUILT[debug_stop]


def _host_inputs(inputs):
    f32 = np.float32
    x_full = np.asarray(inputs["L0_attn_hidden_states"], dtype=f32)
    wq = np.asarray(inputs["L0_attn_Wq"], dtype=f32)
    wqT = np.ascontiguousarray(wq.T)
    shared = {
        "wkT": np.ascontiguousarray(np.asarray(inputs["L0_attn_Wk"], f32).T),
        "wvT": np.ascontiguousarray(np.asarray(inputs["L0_attn_Wv"], f32).T),
        "woT": np.ascontiguousarray(np.asarray(inputs["L0_attn_Wo"], f32).T),
        "bk": np.ascontiguousarray(inputs["L0_attn_bk"].reshape(1, -1).astype(f32)),
        "bv": np.ascontiguousarray(inputs["L0_attn_bv"].reshape(1, -1).astype(f32)),
        "bo": np.ascontiguousarray(inputs["L0_attn_bo"].reshape(1, -1).astype(f32)),
        "rT2": np.ascontiguousarray(
            np.concatenate([np.asarray(inputs["L0_attn_rand_matrix"], f32).T] * 2, axis=1)),
        "rwT": np.ascontiguousarray(np.asarray(inputs["L0_ffn_router_w"], f32).T),
        "gw": np.ascontiguousarray(np.asarray(inputs["L0_ffn_gate_w"], f32)),
        "uw": np.ascontiguousarray(np.asarray(inputs["L0_ffn_up_w"], f32)),
        "dw": np.ascontiguousarray(np.asarray(inputs["L0_ffn_down_w"], f32)),
        "sgw": np.ascontiguousarray(np.asarray(inputs["L0_ffn_shared_gate_w"], f32)),
        "suw": np.ascontiguousarray(np.asarray(inputs["L0_ffn_shared_up_w"], f32)),
        "sdw": np.ascontiguousarray(np.asarray(inputs["L0_ffn_shared_down_w"], f32)),
        "ident": np.eye(128, dtype=ml_dtypes.bfloat16),
        "ones_r": np.ones((1, 128), dtype=ml_dtypes.bfloat16),
    }
    bq_full = inputs["L0_attn_bq"].reshape(1, -1).astype(f32)
    eind = np.zeros((E, E * 128), dtype=ml_dtypes.bfloat16)
    for e in range(E):
        eind[e, e * 128:(e + 1) * 128] = 1
    shared["e_ind"] = eind
    ccn = np.zeros((128, 8), dtype=f32)
    ccn[:, 0] = -0.5
    ccn[64:128, 2] = np.pi / 2
    ccn[0:8, 3] = 1.0 / 64.0
    ccn[:, 4] = 1e-5
    shared["cconst"] = ccn

    inv_freq = 1.0 / (10000.0 ** (np.arange(0, HD, 2, dtype=np.float64) / HD))
    pos = np.arange(NTOK, dtype=np.float64)
    freqs = pos[:, None] * inv_freq[None, :]
    emb = np.concatenate([freqs, freqs], axis=1)
    shared["cos_t"] = np.cos(emb).astype(f32)
    sin_half = np.sin(freqs).astype(f32)
    shared["srot_t"] = np.concatenate([-sin_half, sin_half], axis=1).astype(f32)

    xT_b = [np.ascontiguousarray(x_full[b].T) for b in range(4)]
    in_maps = []
    for c in range(NCORES):
        b = c // 2
        hg = c % 2
        m = dict(shared)
        m["xT"] = xT_b[b]
        m["xr"] = np.ascontiguousarray(x_full[b, hg * NR:(hg + 1) * NR, :])
        m["wqT"] = np.ascontiguousarray(wqT[:, hg * 1024:(hg + 1) * 1024])
        m["bq"] = np.ascontiguousarray(bq_full[:, hg * 1024:(hg + 1) * 1024])
        in_maps.append(m)
    return in_maps


def kernel(debug_stop=None, **inputs):
    in_maps = _host_inputs(inputs)
    nc, _es = _get_program(debug_stop)
    res = run_bass_kernel_spmd(nc, in_maps, core_ids=list(range(NCORES)))
    out = np.empty((4, 2048, D), dtype=np.float32)
    for c in range(NCORES):
        b = c // 2
        hg = c % 2
        out[b, hg * NR:(hg + 1) * NR, :] = res.results[c]["out"]
    return out


# version 13
# speedup vs baseline: 1.0218x; 1.0218x over previous
import numpy as np
import ml_dtypes
import concourse.bass as bass
import concourse.mybir as mybir
from concourse.bass_utils import run_bass_kernel_spmd

F32 = mybir.dt.float32
BF16 = mybir.dt.bfloat16
AF = mybir.ActivationFunctionType
ALU = mybir.AluOpType
AX = mybir.AxisListType

NCORES = 8
NTOK = 2048
NTT = 16
NR = 1024
NRT = 8
D = 2048
HQ = 8
HKV = 8
HD = 128
F_EXP = 1024
E = 16
MAGIC = float(3 * 2.0 ** 22)
TWO_PI = float(2 * np.pi)
INV_2PI = float(1.0 / (2 * np.pi))
CHAINS = [["S"], [0, 1, 2], [3, 4, 5], [6, 7, 8], [9, 10, 11], [12, 13, 14], [15]]


class Ctr:
    def __init__(self, nc, es, name):
        self.h = es.enter_context(nc.semaphore(name))
        self.n = 0

    def inc(self, instr, k=1):
        instr.then_inc(self.h, k)
        self.n += k
        return self.n


def build_program(debug_stop=None):
    from contextlib import ExitStack
    nc = bass.Bass("TRN2", target_bir_lowering=False)
    es = ExitStack()

    def din(name, shape, dtype=F32):
        return nc.dram_tensor(name, list(shape), dtype, kind="ExternalInput").ap()

    xT_d = din("xT", (D, NTOK))
    xr_d = din("xr", (NR, D))
    wqT_d = din("wqT", (D, HQ * HD))
    wkT_d = din("wkT", (D, HKV * HD))
    wvT_d = din("wvT", (D, HKV * HD))
    woT_d = din("woT", (D, D))
    bq_d = din("bq", (1, HQ * HD))
    bk_d = din("bk", (1, HKV * HD))
    bv_d = din("bv", (1, HKV * HD))
    bo_d = din("bo", (1, D))
    rT2_d = din("rT2", (128, 128))
    rwT_d = din("rwT", (D, E))
    gw_d = din("gw", (E, D, F_EXP))
    uw_d = din("uw", (E, D, F_EXP))
    dw_d = din("dw", (E, F_EXP, D))
    sgw_d = din("sgw", (D, D))
    suw_d = din("suw", (D, D))
    sdw_d = din("sdw", (D, D))
    cos_d = din("cos_t", (NTOK, HD), BF16)
    srot_d = din("srot_t", (NTOK, HD), BF16)
    ident_d = din("ident", (128, 128), BF16)
    ones_d = din("ones_r", (1, 128), BF16)
    eind_d = din("e_ind", (E, E * 128), BF16)
    cc_d = din("cconst", (128, 8))
    out_d = nc.dram_tensor("out", [NR, D], F32, kind="ExternalOutput").ap()

    sbt = nc.sbuf_tensor
    ident = es.enter_context(sbt("ident_s", [128, 128], BF16))
    ones_r = es.enter_context(sbt("ones_s", [1, 128], BF16))
    cc = es.enter_context(sbt("cc", [128, 8], F32))
    rT2_sb = es.enter_context(sbt("rT2_sb", [128, 128], BF16))
    rwT_sb = es.enter_context(sbt("rwT_sb", [128, 16 * E], BF16))
    A_T = es.enter_context(sbt("A_T", [128, 128], BF16))
    tbl = es.enter_context(sbt("tbl", [128, 512], BF16))
    ZB = es.enter_context(sbt("ZB", [128, 256], F32))
    REG_G = es.enter_context(sbt("REG_G", [128, 4096], BF16))

    ACCR = es.enter_context(sbt("ACCR", [128, 16384], BF16))
    R1 = es.enter_context(sbt("R1", [128, 16384], BF16))
    R2 = es.enter_context(sbt("R2", [128, 16384], BF16))
    R3 = es.enter_context(sbt("R3", [128, 16384], BF16))
    R4 = es.enter_context(sbt("R4", [128, 16384], BF16))
    R5 = es.enter_context(sbt("R5", [128, 16384], BF16))

    kp = ACCR
    ACC = ACCR
    xTh = R1
    qp = R1
    sqb1 = R1[:, 0:2048]
    outf = R1[:, :].bitcast(F32)
    q_tok = R2
    R2F = R2[:, :].bitcast(F32)
    OUTt = R2F[:, 6144:7168]
    pscr = [R2[:, 14336 + i * 1024: 14336 + (i + 1) * 1024] for i in range(2)]
    x1T = R2
    k_tok = R3
    R3F = R3[:, :].bitcast(F32)
    wo_ring = [R3[:, i * 512:(i + 1) * 512] for i in range(8)]
    x_ring = [R3[:, 4096 + i * 2048: 4096 + (i + 1) * 2048] for i in range(2)]
    bo_sb = R3[0:1, 8192:10240]
    sqb2 = R3[:, 0:2048]
    v_tok = R4
    gu_slot = [R4[:, 8192 + i * 2048: 8192 + (i + 1) * 2048] for i in range(4)]
    wstage = R5
    outT = R5
    outTv = R5[:, :].rearrange("p (ti r) -> p ti r", ti=16)
    dw_slot = [R5[:, i * 4096:(i + 1) * 4096] for i in range(2)]
    wbc_sb = R5[:, 8192:9216]
    wrt = R5[0:16, 9216:10240]
    eind = R5[0:16, 10240:12288]
    mrot2 = [REG_G[:, 0:1024], REG_G[:, 1024:2048]]
    sqn2 = [REG_G[:, 2048:3072], REG_G[:, 3072:4096]]
    ocast = REG_G[:, 0:1024]
    sqb1g = REG_G[:, 1024:3072]
    A_sb = REG_G[0:8, 2048:3072]
    wcast = REG_G[:, 3072:3088]

    POOL = es.enter_context(nc.psum_tensor("POOL", [128, 4096], F32))

    def bank(i):
        return POOL[:, i * 512:(i + 1) * 512]

    TRB = [bank(7).bitcast(BF16)[:, 0:128], bank(2).bitcast(BF16)[:, 0:128]]
    psATs = [bank(4).bitcast(BF16)[:, 0:64], bank(3).bitcast(BF16)[:, 0:64]]
    psA0 = POOL[0:8, 3072:3584]
    psA1 = POOL[0:8, 2560:3072]

    dW = Ctr(nc, es, "dW")
    dS = Ctr(nc, es, "dS")
    cT = Ctr(nc, es, "cT")
    cV = Ctr(nc, es, "cV")
    cS = Ctr(nc, es, "cS")
    cP = Ctr(nc, es, "cP")

    t_, v_, s_, g_, y_ = nc.tensor, nc.vector, nc.scalar, nc.gpsimd, nc.sync

    def gdma(dst, src, wait=None):
        if wait is not None:
            g_.wait_ge(wait[0].h, wait[1])
        return dW.inc(g_.dma_start(dst, src), 16)

    def r3ap(dram_ap, p=128):
        return dram_ap.rearrange("(c p) n -> p c n", p=p)

    def c3(sbuf_ap, c):
        return sbuf_ap.rearrange("p (c n) -> p c n", c=c)

    def mmul(out, lhsT, rhs, start, stop):
        return t_.matmul(out, lhsT, rhs, start=start, stop=stop)

    def dump_and_finish(src_fn, gate_ctr, gate_val):
        g_.wait_ge(gate_ctr.h, gate_val)
        fin = 0
        for rt in range(NRT):
            fin = dW.inc(g_.dma_start(out_d[rt * 128:(rt + 1) * 128, :],
                                      src_fn(rt)), 16)
        g_.wait_ge(dW.h, fin)
        return nc, es

    gdma(ident[:, :], ident_d)
    gdma(ones_r[:, :], ones_d)
    gdma(cc[:, :], cc_d)
    gdma(rT2_sb[:, :], rT2_d)
    gdma(c3(rwT_sb[:, :], 16), r3ap(rwT_d))

    PIECES = ([(wqT_d[:, og * 512:(og + 1) * 512], "q", og) for og in range(2)]
              + [(wkT_d[:, og * 512:(og + 1) * 512], "k", og) for og in range(2)]
              + [(wvT_d[:, og * 512:(og + 1) * 512], "v", og) for og in range(2)])
    bq_sb = ACCR[0:1, 0:1024]
    bk_sb = ACCR[0:1, 1024:2048]
    bv_sb = ACCR[0:1, 2048:3072]
    gdma(bq_sb, bq_d)
    gdma(bk_sb, bk_d)
    gdma(bv_sb, bv_d)

    xT_wait = [None, None]
    xT_wait[0] = gdma(c3(xTh[:, :], 16), r3ap(xT_d[:, 0:1024]))
    qkv_wait = {}

    def load_piece(sh, i, credit):
        half = wstage[:, (i % 2) * 8192:(i % 2) * 8192 + 8192]
        qkv_wait[(sh, i)] = gdma(c3(half, 16), r3ap(PIECES[i][0]), wait=credit)

    load_piece(0, 0, None)
    load_piece(0, 1, None)

    psum_credit = {}

    def qkv_piece(sh, i):
        dsrc, kind, og = PIECES[i]
        half = wstage[:, (i % 2) * 8192:(i % 2) * 8192 + 8192]
        t_.wait_ge(dW.h, qkv_wait[(sh, i)])
        brow = (bq_sb if kind == "q" else bk_sb if kind == "k" else bv_sb)[
            0:1, og * 512:(og + 1) * 512]
        for tile in range(8):
            gt = sh * 8 + tile
            slot = tile % 2
            pq = bank(slot)
            need = psum_credit.get(("p01", slot))
            if need is not None:
                t_.wait_ge(cS.h, need)
            for k in range(16):
                mmul(pq, xTh[:, k * 1024 + tile * 128: k * 1024 + tile * 128 + 128],
                     half[:, k * 512:(k + 1) * 512], start=(k == 0), stop=False)
            te_n = cT.inc(mmul(pq, ones_r[0:1, :], brow, start=False, stop=True))
            s_.wait_ge(cT.h, te_n)
            dst = (q_tok if kind == "q" else k_tok if kind == "k" else v_tok)[
                :, gt * 1024 + og * 512: gt * 1024 + og * 512 + 512]
            psum_credit[("p01", slot)] = cS.inc(s_.copy(dst, pq))

    t_.wait_ge(dW.h, xT_wait[0])
    for i in range(6):
        qkv_piece(0, i)
        if i + 2 < 6:
            load_piece(0, i + 2, (cT, cT.n))
        else:
            load_piece(1, i + 2 - 6, (cT, cT.n))
    xT_wait[1] = gdma(c3(xTh[:, :], 16), r3ap(xT_d[:, 1024:2048]), wait=(cT, cT.n))
    qkv_sh0_sc = cS.n

    tr_credit = [None, None]
    rope_prev = [0, 0]
    v_.wait_ge(cS.h, qkv_sh0_sc)
    g_.wait_ge(cS.h, qkv_sh0_sc)
    tbl_wait = [None] * NTT
    tbl_credit = [None, None]

    def issue_tbl(tile):
        slot = tile % 2
        if tbl_credit[slot] is not None:
            y_.wait_ge(tbl_credit[slot][0].h, tbl_credit[slot][1])
        dS.inc(y_.dma_start(tbl[:, slot * 256: slot * 256 + 128],
                            cos_d[tile * 128:(tile + 1) * 128, :]), 16)
        tbl_wait[tile] = dS.inc(y_.dma_start(
            tbl[:, slot * 256 + 128: slot * 256 + 256],
            srot_d[tile * 128:(tile + 1) * 128, :]), 16)
        tbl_wait[tile] = dS.n

    issue_tbl(0)
    issue_tbl(1)

    def rope_stage_a(tile, kind):
        nh = 8
        w = 1024
        src = (q_tok if kind == "q" else k_tok)[:, tile * 1024:(tile + 1) * 1024]
        src3 = src.rearrange("p (h d) -> p h d", h=nh)
        slot_t = tile % 2
        cosb = tbl[:, slot_t * 256: slot_t * 256 + 128]
        srotb = tbl[:, slot_t * 256 + 128: slot_t * 256 + 256]
        half = 0 if kind == "q" else 1
        m3 = mrot2[half].rearrange("p (h d) -> p h d", h=nh)
        qcos = sqn2[half].rearrange("p (h d) -> p h d", h=nh)
        if half == 0:
            v_.wait_ge(dS.h, tbl_wait[tile])
            g_.wait_ge(dS.h, tbl_wait[tile])
        g_.wait_ge(cV.h, rope_prev[half])
        g_.tensor_tensor(m3[:, :, 0:64], src3[:, :, 64:128],
                         srotb[:, 0:64].unsqueeze(1).broadcast_to((128, nh, 64)), ALU.mult)
        np2 = cP.inc(g_.tensor_tensor(m3[:, :, 64:128], src3[:, :, 0:64],
                     srotb[:, 64:128].unsqueeze(1).broadcast_to((128, nh, 64)), ALU.mult))
        n3 = cV.inc(v_.tensor_tensor(qcos, src3,
                                     cosb.unsqueeze(1).broadcast_to((128, nh, HD)), ALU.mult))
        return (src, src3, half, np2, n3)

    def rope_stage_b(st):
        src, src3, half, np2, n3 = st
        v_.wait_ge(cV.h, n3)
        v_.wait_ge(cP.h, np2)
        n4 = cV.inc(v_.tensor_tensor(src, sqn2[half], mrot2[half], ALU.add))
        rope_prev[half] = n4
        return (src, src3, half, n4)

    def norm_stage_a(st):
        src, src3, half, n4 = st
        zb0 = 0 if half == 0 else 8
        n_sq = None
        for h in range(8):
            n_sq = v_.scalar_tensor_tensor(
                sqn2[half][:, h * 128:(h + 1) * 128], src3[:, h, :], 1.0,
                src3[:, h, :], ALU.mult, ALU.mult,
                accum_out=ZB[:, zb0 + h: zb0 + h + 1])
        n_sq = cV.inc(n_sq)
        return (src, src3, half, n_sq)

    def norm_stage_b(st):
        src, src3, half, n_sq = st
        zb0 = 0 if half == 0 else 8
        v_.wait_ge(cV.h, n_sq)
        n_cl = cV.inc(v_.tensor_scalar(ZB[:, 32 + zb0:40 + zb0], ZB[:, zb0:zb0 + 8],
                                       1e-12, None, ALU.max))
        s_.wait_ge(cV.h, n_cl)
        s_.activation(ZB[:, 64 + zb0:72 + zb0], ZB[:, 32 + zb0:40 + zb0], AF.Ln)
        n_r = cS.inc(s_.activation(ZB[:, 32 + zb0:40 + zb0], ZB[:, 64 + zb0:72 + zb0],
                                   AF.Exp, scale=cc[:, 0:1]))
        v_.wait_ge(cS.h, n_r)
        n_hat = cV.inc(v_.tensor_tensor(
            src3, src3,
            ZB[:, 32 + zb0:40 + zb0].unsqueeze(2).broadcast_to((128, 8, 128)),
            ALU.mult))
        return (src, src3, half, n_hat)

    def trans_stage(st, tile):
        src, src3, half, n_hat = st
        t_.wait_ge(cV.h, n_hat)
        dstb = qp if half == 0 else kp
        for h in range(8):
            sl = h % 2
            if tr_credit[sl] is not None:
                t_.wait_ge(cS.h, tr_credit[sl])
            te1 = cT.inc(t_.transpose(TRB[sl], src3[:, h, :], ident[:, :]))
            s_.wait_ge(cT.h, te1)
            tr_credit[sl] = cS.inc(
                s_.copy(dstb[:, h * 2048 + tile * 128: h * 2048 + tile * 128 + 128],
                        TRB[sl]))

    rope_states = {}

    def rope_norm(tile):
        sq = rope_stage_a(tile, "q")
        sk = rope_stage_a(tile, "k")
        sq = rope_stage_b(sq)
        sk = rope_stage_b(sk)
        tbl_credit[tile % 2] = (cV, rope_prev[1])
        if tile + 2 < NTT:
            issue_tbl(tile + 2)
        sq = norm_stage_a(sq)
        sk = norm_stage_a(sk)
        rope_states[(tile, "q")] = norm_stage_b(sq)
        rope_states[(tile, "k")] = norm_stage_b(sk)

    def trans_pair(tile):
        trans_stage(rope_states.pop((tile, "q")), tile)
        trans_stage(rope_states.pop((tile, "k")), tile)

    fm_sets = [(R2F[:, 0:1024], R2F[:, 1024:2048]),
               (R3F[:, 0:1024], R3F[:, 1024:2048])]
    fm_vec_done = [None, None]
    fm_bank_credit = {}
    fm_gi = [0]

    def fm_group(base, h, gp2, banks):
        col = h * 2048 + gp2 * 1024
        sel = fm_gi[0] % 2
        bA, bB = banks[sel % len(banks)]
        rp, ra = fm_sets[sel]
        if fm_bank_credit.get((bA, bB)) is not None:
            t_.wait_ge(cS.h, fm_bank_credit[(bA, bB)])
        mmul(bank(bA), rT2_sb[:, :], base[:, col:col + 512],
             start=True, stop=True)
        te = cT.inc(mmul(bank(bB), rT2_sb[:, :], base[:, col + 512:col + 1024],
                         start=True, stop=True))
        s_.wait_ge(cT.h, te)
        if fm_vec_done[sel] is not None:
            s_.wait_ge(cV.h, fm_vec_done[sel])
        s_.activation(rp[:, 0:512], bank(bA), AF.Identity, bias=cc[:, 2:3])
        nev = cS.inc(s_.activation(rp[:, 512:1024], bank(bB), AF.Identity,
                                   bias=cc[:, 2:3]))
        fm_bank_credit[(bA, bB)] = nev
        v_.wait_ge(cS.h, nev)
        nv0 = cV.inc(v_.tensor_scalar(ra, rp, INV_2PI, MAGIC, ALU.mult, ALU.add))
        v_.wait_ge(cV.h, nv0)
        nv1 = cV.inc(v_.tensor_scalar(ra, ra, MAGIC, TWO_PI,
                                      ALU.subtract, ALU.mult))
        v_.wait_ge(cV.h, nv1)
        nv2 = cV.inc(v_.tensor_tensor(ra, rp, ra, ALU.subtract))
        fm_vec_done[sel] = nv2
        s_.wait_ge(cV.h, nv2)
        cS.inc(s_.activation(base[:, col:col + 1024], ra, AF.Sin))
        fm_gi[0] += 1


    rope_sched_B = [[0], [1, 2], [3], [4, 5], [6], [7]]
    t_.wait_ge(dW.h, xT_wait[1])
    qk1_sc = None
    for i in range(6):
        qkv_piece(1, i)
        if i + 2 < 6:
            load_piece(1, i + 2, (cT, cT.n))
        if i == 3:
            qk1_sc = cS.n
        for t in rope_sched_B[i]:
            rope_norm(t)
    qkv_done_t = cT.n
    qkv_done_sc = cS.n

    if debug_stop == 1:
        return dump_and_finish(lambda rt: q_tok[:, rt * 2048: rt * 2048 + 2048],
                               cS, qkv_done_sc)

    v_.wait_ge(cS.h, qk1_sc)
    g_.wait_ge(cS.h, qk1_sc)
    t_.wait_ge(cS.h, qkv_done_sc)
    fm_list = [(base, h) for base in (qp, kp) for h in range(8)]
    for j in range(8):
        trans_pair(j)
        rope_norm(8 + j)
    transC1_sc = cS.n
    t_.wait_ge(cS.h, transC1_sc)
    for j in range(8):
        trans_pair(8 + j)
        for (fb, fh) in fm_list[j * 2:(j + 1) * 2]:
            fm_group(fb, fh, 0, [(6, 5), (4, 3)])
    qhat_done_sc = cS.n
    fm_gp0_sc = cS.n

    if debug_stop == 21:
        return dump_and_finish(lambda rt: qp[:, rt * 2048: rt * 2048 + 2048],
                               cS, qhat_done_sc)
    if debug_stop == 23:
        return dump_and_finish(lambda rt: kp[:, rt * 2048: rt * 2048 + 2048],
                               cS, qhat_done_sc)


    P3 = {"prev_scA": None, "prev_mm2": {0: None, 1: None}}

    def phase3_tile(tile):
        prev_scA = P3["prev_scA"]
        prev_mm2 = P3["prev_mm2"]
        ats = tile % 2
        last = None
        for half in range(2):
            pA = psA0 if half == 0 else psA1
            for tl in range(64):
                tglob = tile * 128 + half * 64 + tl
                if half == 0 and tl == 0 and prev_scA is not None:
                    t_.wait_ge(cS.h, prev_scA)
                qsl = qp[:, tglob: tglob + 7 * 2048 + 1: 2048]
                ksl = kp[:, tglob: tglob + 7 * 2048 + 1: 2048]
                outap = pA[:, tl: tl + 7 * 64 + 1: 64]
                last = mmul(outap, qsl, ksl, start=True, stop=True)
        teA = cT.inc(last)
        s_.wait_ge(cT.h, teA)
        s_.activation(A_sb[:, 0:512], psA0, AF.Copy, scale=cc[0:8, 3:4])
        prev_scA = cS.inc(s_.activation(A_sb[:, 512:1024], psA1, AF.Copy,
                                        scale=cc[0:8, 3:4]))
        t_.wait_ge(cS.h, prev_scA)
        P3["prev_scA"] = prev_scA
        if prev_mm2[ats] is not None:
            t_.wait_ge(cV.h, prev_mm2[ats])
        lastT = None
        for half in range(2):
            for hp in range(HKV):
                lastT = t_.transpose(
                    psATs[ats][half * 64: half * 64 + 64, hp * 8:(hp + 1) * 8],
                    A_sb[:, half * 512 + hp * 64: half * 512 + (hp + 1) * 64],
                    ident[0:8, 0:8])
        teT = cT.inc(lastT)
        v_.wait_ge(cT.h, teT)
        atv = A_T[:, ats * 64:(ats + 1) * 64]
        nvat = cV.inc(v_.tensor_copy(atv, psATs[ats]))
        v_.wait_ge(cV.h, nvat)
        at3 = atv.rearrange("p (hp h) -> p hp h", hp=HKV)
        prodn = [None, None]
        addn = {}
        for hp in range(HKV):
            sl2 = hp % 2
            if hp >= 2:
                v_.wait_ge(cV.h, addn[hp - 2])
            vb = v_tok[:, tile * 1024 + hp * 128: tile * 1024 + hp * 128 + 128]
            prodn[sl2] = cV.inc(v_.tensor_tensor(
                pscr[sl2].rearrange("p (h d) -> p h d", h=HQ),
                vb.unsqueeze(1).broadcast_to((128, HQ, 128)),
                at3[:, hp, :].unsqueeze(2).broadcast_to((128, HQ, 128)),
                ALU.mult))
            v_.wait_ge(cV.h, prodn[sl2])
            if hp == 0:
                addn[hp] = cV.inc(v_.tensor_copy(OUTt, pscr[sl2]))
            else:
                v_.wait_ge(cV.h, addn[hp - 1])
                addn[hp] = cV.inc(v_.tensor_tensor(OUTt, OUTt, pscr[sl2], ALU.add))
        edge = addn[HKV - 1]
        prev_mm2[ats] = edge
        v_.wait_ge(cV.h, edge)
        ncast = cV.inc(v_.tensor_copy(ocast, OUTt))
        t_.wait_ge(cV.h, ncast)
        for h in range(HQ):
            sl = h % 2
            if tr_credit[sl] is not None:
                t_.wait_ge(cS.h, tr_credit[sl])
            te2 = cT.inc(t_.transpose(TRB[sl], ocast[:, h * 128:(h + 1) * 128],
                                      ident[:, :]))
            s_.wait_ge(cT.h, te2)
            tr_credit[sl] = cS.inc(s_.copy(
                outTv[:, :, h * 128 + tile * 8: h * 128 + tile * 8 + 8],
                TRB[sl].rearrange("p (tb ti) -> p ti tb", tb=8)))

    fm_bank_credit[(0, 1)] = max(psum_credit.get(("p01", 0), 0),
                                 psum_credit.get(("p01", 1), 0))
    s_.wait_ge(cS.h, fm_gp0_sc)
    t_.wait_ge(cS.h, fm_gp0_sc)
    v_.wait_ge(cS.h, fm_gp0_sc)
    for j in range(8):
        phase3_tile(j)
        for (fb, fh) in fm_list[j * 2:(j + 1) * 2]:
            fm_group(fb, fh, 1, [(0, 1)])
    featmap_sc = cS.n

    if debug_stop == 2:
        return dump_and_finish(lambda rt: qp[:, rt * 2048: rt * 2048 + 2048],
                               cS, featmap_sc)
    if debug_stop == 22:
        return dump_and_finish(lambda rt: kp[:, rt * 2048: rt * 2048 + 2048],
                               cS, featmap_sc)

    t_.wait_ge(cS.h, featmap_sc)
    v_.wait_ge(cS.h, featmap_sc)
    for tile in range(8, NTT):
        phase3_tile(tile)
    outT_sc = cS.n
    phase3_v = cV.n
    phase3_t = cT.n

    if debug_stop == 3:
        return dump_and_finish(lambda rt: outT[:, rt * 2048: rt * 2048 + 2048],
                               cS, outT_sc)

    wo_wait = {}
    wo_slot_credit = {}
    pieces4 = [(oq, k) for oq in range(4) for k in range(16)]
    np_issued = 0

    def issue_wo(n):
        nonlocal np_issued
        while np_issued < min(n, 64):
            oq, k = pieces4[np_issued]
            slot = np_issued % 8
            credit = wo_slot_credit.get(slot)
            if np_issued == 0:
                credit = (cS, featmap_sc)
            wo_wait[(oq, k)] = gdma(
                wo_ring[slot][:, :],
                woT_d[k * 128:(k + 1) * 128, oq * 512:(oq + 1) * 512], wait=credit)
            np_issued += 1

    issue_wo(8)
    bo_wait = gdma(bo_sb, bo_d)
    wo_bank_credit = [
        (cS, psum_credit.get(("p01", 0), 0)), (cS, psum_credit.get(("p01", 1), 0)),
        (cS, outT_sc), (cS, 0), (cV, phase3_v), (cS, P3["prev_scA"] or 0),
        (cS, P3["prev_scA"] or 0), (cS, outT_sc)]
    for oq in range(4):
        tile_done = [None] * NRT
        for k in range(16):
            t_.wait_ge(dW.h, wo_wait[(oq, k)])
            if k == 0:
                for b8 in range(8):
                    need = wo_bank_credit[b8]
                    if need is not None:
                        t_.wait_ge(need[0].h, need[1])
                        wo_bank_credit[b8] = None
            lastk = None
            for rt in range(NRT):
                lastk = mmul(bank(rt),
                             outT[:, k * 1024 + rt * 128: k * 1024 + rt * 128 + 128],
                             wo_ring[(oq * 16 + k) % 8][:, :],
                             start=(k == 0), stop=False)
            wo_slot_credit[(oq * 16 + k) % 8] = (cT, cT.inc(lastk))
            issue_wo(oq * 16 + k + 9)
        t_.wait_ge(dW.h, bo_wait)
        for rt in range(NRT):
            tile_done[rt] = cT.inc(
                mmul(bank(rt), ones_r[0:1, :], bo_sb[0:1, oq * 512:(oq + 1) * 512],
                     start=False, stop=True))
        for rt in range(NRT):
            s_.wait_ge(cT.h, tile_done[rt])
            nsc = cS.inc(
                s_.copy(ACC[:, rt * 2048 + oq * 512: rt * 2048 + oq * 512 + 512],
                        bank(rt)))
            wo_bank_credit[rt] = (cS, nsc)
    acc_attn_sc = cS.n
    wo_done_t = cT.n

    x_wait = [None] * NRT
    x_slot_credit = [(cV, phase3_v), (cV, phase3_v)]

    def issue_x(rt):
        x_wait[rt] = gdma(x_ring[rt % 2][:, :],
                          xr_d[rt * 128:(rt + 1) * 128, :],
                          wait=x_slot_credit[rt % 2])

    issue_x(0)
    issue_x(1)
    v_.wait_ge(cS.h, acc_attn_sc)
    pl = POOL[:, 2048:2064]
    pl_credit = None
    router_done_sc = None
    for rt in range(NRT):
        at = ACC[:, rt * 2048:(rt + 1) * 2048]
        v_.wait_ge(dW.h, x_wait[rt])
        nres = cV.inc(v_.tensor_tensor(at, at, x_ring[rt % 2][:, :], ALU.add))
        x_slot_credit[rt % 2] = (cV, nres)
        if rt + 2 < NRT:
            issue_x(rt + 2)
        s_.wait_ge(cV.h, nres)
        nsq = cS.inc(s_.activation(sqb1, at, AF.Square))
        v_.wait_ge(cS.h, nsq)
        v_.tensor_reduce(ZB[:, 0:1], at, AX.X, ALU.add)
        nrd = cV.inc(v_.tensor_reduce(ZB[:, 1:2], sqb1, AX.X, ALU.add))
        v_.wait_ge(cV.h, nrd)
        v_.tensor_scalar(ZB[:, 2:3], ZB[:, 0:1], 1.0 / D, None, ALU.mult)
        nv1 = cV.inc(v_.tensor_scalar(ZB[:, 3:4], ZB[:, 1:2], 1.0 / D, None, ALU.mult))
        v_.wait_ge(cV.h, nv1)
        nv2 = cV.inc(v_.tensor_tensor(ZB[:, 4:5], ZB[:, 2:3], ZB[:, 2:3], ALU.mult))
        v_.wait_ge(cV.h, nv2)
        nvar = cV.inc(v_.tensor_tensor(ZB[:, 5:6], ZB[:, 3:4], ZB[:, 4:5], ALU.subtract))
        s_.wait_ge(cV.h, nvar)
        s_.activation(ZB[:, 6:7], ZB[:, 5:6], AF.Ln, bias=cc[:, 4:5])
        nrstd = cS.inc(s_.activation(ZB[:, 7:8], ZB[:, 6:7], AF.Exp, scale=cc[:, 0:1]))
        v_.wait_ge(cS.h, nrstd)
        nx1 = cV.inc(v_.tensor_scalar(at, at, ZB[:, 2:3], ZB[:, 7:8],
                                      ALU.subtract, ALU.mult))
        t_.wait_ge(cV.h, nx1)
        for c in range(16):
            sl = c % 2
            if tr_credit[sl] is not None:
                t_.wait_ge(cS.h, tr_credit[sl])
            te2 = cT.inc(t_.transpose(TRB[sl], at[:, c * 128:(c + 1) * 128],
                                      ident[:, :]))
            s_.wait_ge(cT.h, te2)
            tr_credit[sl] = cS.inc(
                s_.copy(x1T[:, c * 1024 + rt * 128: c * 1024 + rt * 128 + 128],
                        TRB[sl]))
        t_.wait_ge(cS.h, tr_credit[1])
        if pl_credit is not None:
            t_.wait_ge(cV.h, pl_credit)
        tel = None
        for c in range(16):
            tel = mmul(pl, x1T[:, c * 1024 + rt * 128: c * 1024 + rt * 128 + 128],
                       rwT_sb[:, c * 16:(c + 1) * 16], start=(c == 0), stop=(c == 15))
        tel = cT.inc(tel)
        v_.wait_ge(cT.h, tel)
        lg = ZB[:, 16:32]
        nlg = cV.inc(v_.tensor_copy(lg, pl))
        pl_credit = nlg
        v_.wait_ge(cV.h, nlg)
        nl1 = cV.inc(v_.reduce_max(ZB[:, 32:33], lg, axis=AX.X))
        v_.wait_ge(cV.h, nl1)
        nm1 = cV.inc(v_.tensor_scalar(ZB[:, 48:64], lg, ZB[:, 32:33], None, ALU.is_ge))
        v_.wait_ge(cV.h, nm1)
        nt1 = cV.inc(v_.scalar_tensor_tensor(ZB[:, 64:80], ZB[:, 48:64], -1e30, lg,
                                             ALU.mult, ALU.add))
        v_.wait_ge(cV.h, nt1)
        nl2 = cV.inc(v_.reduce_max(ZB[:, 33:34], ZB[:, 64:80], axis=AX.X))
        v_.wait_ge(cV.h, nl2)
        v_.tensor_scalar(ZB[:, 80:96], lg, ZB[:, 33:34], None, ALU.is_ge)
        ndl = cV.inc(v_.tensor_tensor(ZB[:, 34:35], ZB[:, 32:33], ZB[:, 33:34],
                                      ALU.subtract))
        v_.wait_ge(cV.h, ndl)
        nm2 = cV.inc(v_.tensor_tensor(ZB[:, 96:112], ZB[:, 80:96], ZB[:, 48:64],
                                      ALU.subtract))
        s_.wait_ge(cV.h, nm2)
        nw1 = cS.inc(s_.activation(ZB[:, 35:36], ZB[:, 34:35], AF.Sigmoid))
        v_.wait_ge(cS.h, nw1)
        nw2 = cV.inc(v_.tensor_scalar(ZB[:, 36:37], ZB[:, 35:36], -1.0, 1.0,
                                      ALU.mult, ALU.add))
        v_.wait_ge(cV.h, nw2)
        nwa = cV.inc(v_.tensor_scalar(ZB[:, 64:80], ZB[:, 48:64], ZB[:, 35:36],
                                      None, ALU.mult))
        v_.wait_ge(cV.h, nwa)
        nwr = cV.inc(v_.scalar_tensor_tensor(ZB[:, 80:96], ZB[:, 96:112], ZB[:, 36:37],
                                             ZB[:, 64:80], ALU.mult, ALU.add))
        v_.wait_ge(cV.h, nwr)
        nwb = cV.inc(v_.tensor_copy(wcast, ZB[:, 80:96]))
        t_.wait_ge(cV.h, nwb)
        if tr_credit[0] is not None:
            t_.wait_ge(cS.h, tr_credit[0])
        tew = cT.inc(t_.transpose(TRB[0][0:16, :], wcast, ident[:, :]))
        s_.wait_ge(cT.h, tew)
        tr_credit[0] = cS.inc(
            s_.copy(wrt[0:16, rt * 128:(rt + 1) * 128], TRB[0][0:16, 0:128]))
        router_done_sc = tr_credit[0]
    ln1_done_v = cV.n
    x1T_done_sc = cS.n
    eind_wait = gdma(eind[:, :], eind_d, wait=(cT, wo_done_t))

    if debug_stop == 4:
        return dump_and_finish(lambda rt: ACC[:, rt * 2048:(rt + 1) * 2048],
                               cS, x1T_done_sc)

    t_.wait_ge(cS.h, x1T_done_sc)
    t_.wait_ge(dW.h, eind_wait)
    sil = REG_G[:, 1024:1536]

    def hw_ap(pos, hh):
        if pos < 2:
            return R3[:, (pos * 8 + hh) * 1024:(pos * 8 + hh + 1) * 1024]
        return R4[:, hh * 1024:(hh + 1) * 1024]

    hw_first_guard = {"R3": (cT, wo_done_t), "R4": (cV, phase3_v)}

    GUSEQ = []
    for C in CHAINS:
        for e in C:
            for hh in range(16 if e == "S" else 8):
                GUSEQ.append((e, hh))
    gu_wait = {}
    gu_pair_te = {}

    def issue_gu_pair(i):
        if i >= len(GUSEQ):
            return
        e, hh = GUSEQ[i]
        basep = (2 * i) % 4
        credit = gu_pair_te.get(i - 2)
        if credit is None:
            credit = (cV, phase3_v)
        for j, wsel in enumerate(("g", "u")):
            if e == "S":
                srcw = (sgw_d if wsel == "g" else suw_d)[:, hh * 128:(hh + 1) * 128]
            else:
                srcw = (gw_d if wsel == "g" else uw_d)[e, :, hh * 128:(hh + 1) * 128]
            gu_wait[(e, hh, wsel)] = (basep + j, gdma(
                c3(gu_slot[basep + j], 16), r3ap(srcw),
                wait=(credit if j == 0 else None)))

    DWSEQ = []
    for C in CHAINS:
        for oq in range(4):
            for e in C:
                for hseg in range(2 if e == "S" else 1):
                    DWSEQ.append((e, oq, hseg))
    dw_wait = {}
    dw_seq_te = {}
    dw_idx_of = {key: i for i, key in enumerate(DWSEQ)}

    def issue_dw_piece(i):
        if i >= len(DWSEQ):
            return
        e, oq, hseg = DWSEQ[i]
        slot = i % 2
        credit = dw_seq_te.get(i - 2)
        if credit is None:
            credit = (cT, wo_done_t)
        if e == "S":
            srcw = sdw_d[hseg * 1024:(hseg + 1) * 1024, oq * 512:(oq + 1) * 512]
        else:
            srcw = dw_d[e, :, oq * 512:(oq + 1) * 512]
        dw_wait[(e, oq, hseg)] = (slot, gdma(c3(dw_slot[slot], 8), r3ap(srcw),
                                             wait=credit))

    issue_gu_pair(0)
    issue_gu_pair(1)
    issue_dw_piece(0)
    issue_dw_piece(1)

    ln2_gate = [None] * NRT
    down_bank_credit = [(cS, acc_attn_sc)] * 8
    down_bank_credit[4] = (cV, pl_credit or 0)
    down_bank_credit[2] = (cS, x1T_done_sc)
    down_bank_credit[7] = (cS, x1T_done_sc)
    gu_pair_idx = 0

    for ci, C in enumerate(CHAINS):
        for pos, e in enumerate(C):
            nhh = 16 if e == "S" else 8
            if e != "S":
                for gk in range(2):
                    t_.wait_ge(cS.h, router_done_sc)
                    need = down_bank_credit[4]
                    if need is not None:
                        t_.wait_ge(need[0].h, need[1])
                    tew = cT.inc(mmul(POOL[:, 2048:2560], eind[:, e * 128:(e + 1) * 128],
                                      wrt[0:16, gk * 512:(gk + 1) * 512],
                                      start=True, stop=True))
                    s_.wait_ge(cT.h, tew)
                    nwc = cS.inc(s_.copy(wbc_sb[:, gk * 512:(gk + 1) * 512],
                                         POOL[:, 2048:2560]))
                    down_bank_credit[4] = (cS, nwc)
            gkey = "R4" if (e != "S" and pos == 2) else "R3"
            guard = hw_first_guard.get(gkey)
            if guard is not None:
                v_.wait_ge(guard[0].h, guard[1])
                hw_first_guard[gkey] = None
            for hh in range(nhh):
                gslot, gw_w = gu_wait[(e, hh, "g")]
                uslot, uw_w = gu_wait[(e, hh, "u")]
                t_.wait_ge(dW.h, max(gw_w, uw_w))
                for gk in range(2):
                    pg = bank(gk)
                    pu = bank(2 + gk)
                    for bsel in (gk, 2 + gk):
                        need = down_bank_credit[bsel]
                        if need is not None:
                            t_.wait_ge(need[0].h, need[1])
                            down_bank_credit[bsel] = None
                    teg = None
                    for k in range(16):
                        mmul(pg, gu_slot[gslot][:, k * 128:(k + 1) * 128],
                             x1T[:, k * 1024 + gk * 512: k * 1024 + gk * 512 + 512],
                             start=(k == 0), stop=(k == 15))
                    for k in range(16):
                        teg = mmul(pu, gu_slot[uslot][:, k * 128:(k + 1) * 128],
                                   x1T[:, k * 1024 + gk * 512: k * 1024 + gk * 512 + 512],
                                   start=(k == 0), stop=(k == 15))
                    teg = cT.inc(teg)
                    s_.wait_ge(cT.h, teg)
                    nsil = cS.inc(s_.activation(sil, pg, AF.Silu))
                    down_bank_credit[gk] = (cS, nsil)
                    v_.wait_ge(cS.h, nsil)
                    hcols = hw_ap(pos if e != "S" else hh // 8,
                                  hh if e != "S" else hh % 8)[:, gk * 512:(gk + 1) * 512]
                    nh1 = cV.inc(v_.tensor_tensor(hcols, sil, pu, ALU.mult))
                    down_bank_credit[2 + gk] = (cV, nh1)
                    if e != "S":
                        v_.wait_ge(cV.h, nh1)
                        cV.inc(v_.tensor_tensor(hcols, hcols,
                                                wbc_sb[:, gk * 512:(gk + 1) * 512],
                                                ALU.mult))
                gu_pair_te[gu_pair_idx] = (cT, cT.n)
                issue_gu_pair(gu_pair_idx + 2)
                gu_pair_idx += 1
        nhw_done = cV.n
        for oq in range(4):
            tile_mm_done = [None] * 8
            for pi, e in enumerate(C):
                nhh = 16 if e == "S" else 8
                first_member = (pi == 0)
                last_member = (pi == len(C) - 1)
                seg_last = {}
                for hh in range(nhh):
                    hseg = hh // 8 if e == "S" else 0
                    slot, wv = dw_wait[(e, oq, hseg)]
                    t_.wait_ge(dW.h, wv)
                    if first_member and hh == 0:
                        t_.wait_ge(cV.h, nhw_done)
                        for b8 in range(8):
                            need = down_bank_credit[b8]
                            if need is not None:
                                t_.wait_ge(need[0].h, need[1])
                                down_bank_credit[b8] = None
                    rhs = dw_slot[slot][:, (hh % 8) * 512:((hh % 8) + 1) * 512]
                    lhs_pos = (hh // 8) if e == "S" else pi
                    lhs_hh = (hh % 8) if e == "S" else hh
                    for rt in range(NRT):
                        mmn = mmul(bank(rt),
                                   hw_ap(lhs_pos, lhs_hh)[:, rt * 128:(rt + 1) * 128],
                                   rhs,
                                   start=(first_member and hh == 0),
                                   stop=(last_member and hh == nhh - 1))
                        if last_member and hh == nhh - 1:
                            tile_mm_done[rt] = cT.inc(mmn)
                        else:
                            seg_last[hseg] = mmn
                    if (hh == nhh - 1) or (e == "S" and hh == 7):
                        idx = dw_idx_of[(e, oq, hseg)]
                        if last_member and hh == nhh - 1:
                            mdone = cT.n
                        else:
                            mdone = cT.inc(seg_last[hseg])
                        dw_seq_te[idx] = (cT, mdone)
                        issue_dw_piece(idx + 2)
            for rt in range(NRT):
                v_.wait_ge(cT.h, tile_mm_done[rt])
                nev = cV.inc(v_.tensor_tensor(
                    ACC[:, rt * 2048 + oq * 512: rt * 2048 + oq * 512 + 512],
                    ACC[:, rt * 2048 + oq * 512: rt * 2048 + oq * 512 + 512],
                    bank(rt), ALU.add))
                down_bank_credit[rt] = (cV, nev)
                if ci == len(CHAINS) - 1 and oq == 3:
                    ln2_gate[rt] = nev
        if debug_stop == 51 + ci:
            return dump_and_finish(lambda rt: ACC[:, rt * 2048:(rt + 1) * 2048],
                                   cV, cV.n)
    ffn_done_v = cV.n

    if debug_stop == 5:
        return dump_and_finish(lambda rt: ACC[:, rt * 2048:(rt + 1) * 2048],
                               cV, ffn_done_v)

    out_dma = [None] * NRT
    for rt in range(NRT):
        at = ACC[:, rt * 2048:(rt + 1) * 2048]
        s_.wait_ge(cV.h, ln2_gate[rt])
        nsq = cS.inc(s_.activation(sqb2, at, AF.Square))
        v_.wait_ge(cS.h, nsq)
        v_.tensor_reduce(ZB[:, 0:1], at, AX.X, ALU.add)
        nrd = cV.inc(v_.tensor_reduce(ZB[:, 1:2], sqb2, AX.X, ALU.add))
        v_.wait_ge(cV.h, nrd)
        v_.tensor_scalar(ZB[:, 2:3], ZB[:, 0:1], 1.0 / D, None, ALU.mult)
        nv1 = cV.inc(v_.tensor_scalar(ZB[:, 3:4], ZB[:, 1:2], 1.0 / D, None, ALU.mult))
        v_.wait_ge(cV.h, nv1)
        nv2 = cV.inc(v_.tensor_tensor(ZB[:, 4:5], ZB[:, 2:3], ZB[:, 2:3], ALU.mult))
        v_.wait_ge(cV.h, nv2)
        nvar = cV.inc(v_.tensor_tensor(ZB[:, 5:6], ZB[:, 3:4], ZB[:, 4:5], ALU.subtract))
        s_.wait_ge(cV.h, nvar)
        s_.activation(ZB[:, 6:7], ZB[:, 5:6], AF.Ln, bias=cc[:, 4:5])
        nrstd = cS.inc(s_.activation(ZB[:, 7:8], ZB[:, 6:7], AF.Exp, scale=cc[:, 0:1]))
        v_.wait_ge(cS.h, nrstd)
        slot = rt % 4
        if rt >= 4:
            v_.wait_ge(dS.h, out_dma[rt - 4])
        nout = cV.inc(v_.tensor_scalar(outf[:, slot * 2048:(slot + 1) * 2048], at,
                                       ZB[:, 2:3], ZB[:, 7:8], ALU.subtract, ALU.mult))
        y_.wait_ge(cV.h, nout)
        out_dma[rt] = dS.inc(y_.dma_start(out_d[rt * 128:(rt + 1) * 128, :],
                                          outf[:, slot * 2048:(slot + 1) * 2048]), 16)
    y_.wait_ge(dS.h, out_dma[NRT - 1])
    return nc, es


_BUILT = {}


def _get_program(debug_stop=None):
    if debug_stop not in _BUILT:
        _BUILT[debug_stop] = build_program(debug_stop)
    return _BUILT[debug_stop]


def _host_inputs(inputs):
    f32 = np.float32
    x_full = np.asarray(inputs["L0_attn_hidden_states"], dtype=f32)
    wq = np.asarray(inputs["L0_attn_Wq"], dtype=f32)
    wqT = np.ascontiguousarray(wq.T)
    shared = {
        "wkT": np.ascontiguousarray(np.asarray(inputs["L0_attn_Wk"], f32).T),
        "wvT": np.ascontiguousarray(np.asarray(inputs["L0_attn_Wv"], f32).T),
        "woT": np.ascontiguousarray(np.asarray(inputs["L0_attn_Wo"], f32).T),
        "bk": np.ascontiguousarray(inputs["L0_attn_bk"].reshape(1, -1).astype(f32)),
        "bv": np.ascontiguousarray(inputs["L0_attn_bv"].reshape(1, -1).astype(f32)),
        "bo": np.ascontiguousarray(inputs["L0_attn_bo"].reshape(1, -1).astype(f32)),
        "rT2": np.ascontiguousarray(
            np.concatenate([np.asarray(inputs["L0_attn_rand_matrix"], f32).T] * 2, axis=1)),
        "rwT": np.ascontiguousarray(np.asarray(inputs["L0_ffn_router_w"], f32).T),
        "gw": np.ascontiguousarray(np.asarray(inputs["L0_ffn_gate_w"], f32)),
        "uw": np.ascontiguousarray(np.asarray(inputs["L0_ffn_up_w"], f32)),
        "dw": np.ascontiguousarray(np.asarray(inputs["L0_ffn_down_w"], f32)),
        "sgw": np.ascontiguousarray(np.asarray(inputs["L0_ffn_shared_gate_w"], f32)),
        "suw": np.ascontiguousarray(np.asarray(inputs["L0_ffn_shared_up_w"], f32)),
        "sdw": np.ascontiguousarray(np.asarray(inputs["L0_ffn_shared_down_w"], f32)),
        "ident": np.eye(128, dtype=ml_dtypes.bfloat16),
        "ones_r": np.ones((1, 128), dtype=ml_dtypes.bfloat16),
    }
    bq_full = inputs["L0_attn_bq"].reshape(1, -1).astype(f32)
    eind = np.zeros((E, E * 128), dtype=ml_dtypes.bfloat16)
    for e in range(E):
        eind[e, e * 128:(e + 1) * 128] = 1
    shared["e_ind"] = eind
    ccn = np.zeros((128, 8), dtype=f32)
    ccn[:, 0] = -0.5
    ccn[64:128, 2] = np.pi / 2
    ccn[0:8, 3] = 1.0 / 64.0
    ccn[:, 4] = 1e-5
    shared["cconst"] = ccn

    inv_freq = 1.0 / (10000.0 ** (np.arange(0, HD, 2, dtype=np.float64) / HD))
    pos = np.arange(NTOK, dtype=np.float64)
    freqs = pos[:, None] * inv_freq[None, :]
    emb = np.concatenate([freqs, freqs], axis=1)
    shared["cos_t"] = np.cos(emb).astype(ml_dtypes.bfloat16)
    sin_half = np.sin(freqs).astype(f32)
    shared["srot_t"] = np.concatenate([-sin_half, sin_half],
                                      axis=1).astype(ml_dtypes.bfloat16)

    xT_b = [np.ascontiguousarray(x_full[b].T) for b in range(4)]
    in_maps = []
    for c in range(NCORES):
        b = c // 2
        hg = c % 2
        m = dict(shared)
        m["xT"] = xT_b[b]
        m["xr"] = np.ascontiguousarray(x_full[b, hg * NR:(hg + 1) * NR, :])
        m["wqT"] = np.ascontiguousarray(wqT[:, hg * 1024:(hg + 1) * 1024])
        m["bq"] = np.ascontiguousarray(bq_full[:, hg * 1024:(hg + 1) * 1024])
        in_maps.append(m)
    return in_maps


def kernel(debug_stop=None, **inputs):
    in_maps = _host_inputs(inputs)
    nc, _es = _get_program(debug_stop)
    res = run_bass_kernel_spmd(nc, in_maps, core_ids=list(range(NCORES)))
    out = np.empty((4, 2048, D), dtype=np.float32)
    for c in range(NCORES):
        b = c // 2
        hg = c % 2
        out[b, hg * NR:(hg + 1) * NR, :] = res.results[c]["out"]
    return out
